# version 01
# baseline (speedup 1.0000x reference)
import numpy as np
import ml_dtypes
import concourse.bass as bass
import concourse.mybir as mybir
from concourse.bass_utils import run_bass_kernel_spmd

F32 = mybir.dt.float32
BF16 = mybir.dt.bfloat16
AF = mybir.ActivationFunctionType
ALU = mybir.AluOpType
BF = ml_dtypes.bfloat16

NCORES = 8
D = 4096
BATCH = 2
SEQ = 4096
NTOK = BATCH * SEQ
TPC = NTOK // NCORES

H = 64
P = 64
N = 128
G = 8
K_CONV = 4
I_DIM = H * P
GN = G * N
CONV_DIM = I_DIM + 2 * GN
PROJ = I_DIM + CONV_DIM + H
EPS = 1e-5


def cdiv(a, b):
    return (a + b - 1) // b


class Sem:

    def __init__(self, stack, nc, name):
        self.h = stack.enter_context(nc.semaphore(name))
        self.v = 0

    def inc(self, instr, n=1):
        instr.then_inc(self.h, n)
        self.v += n
        return self.v

    def wait(self, eng, v=None):
        eng.wait_ge(self.h, self.v if v is None else v)



def build_ln_tm(T):
    from contextlib import ExitStack
    nc = bass.Bass("TRN2", target_bir_lowering=False)
    x = nc.dram_tensor("x", [T, D], F32, kind="ExternalInput").ap()
    xn = nc.dram_tensor("xn", [T, D], BF16, kind="ExternalOutput").ap()
    nt = T // 128
    with ExitStack() as st_:
        xt = st_.enter_context(nc.sbuf_tensor("xt", [128, D], F32))
        xc = st_.enter_context(nc.sbuf_tensor("xc", [128, D], F32))
        sq = st_.enter_context(nc.sbuf_tensor("sq", [128, D], F32))
        xo = st_.enter_context(nc.sbuf_tensor("xo", [128, D], BF16))
        st = st_.enter_context(nc.sbuf_tensor("st", [128, 8], F32))
        ld = Sem(st_, nc, "ld")
        vv = Sem(st_, nc, "vv")
        so = Sem(st_, nc, "so")
        for i in range(nt):
            with nc.Block() as blk:
                @blk.sync
                def _(s, i=i):
                    ld.inc(s.dma_start(xt[:, :], x[i * 128:(i + 1) * 128, :]), 16)
                @blk.vector
                def _(v, i=i):
                    ld.wait(v)
                    vv.inc(v.reduce_sum(st[:, 0:1], xt[:, :], axis=mybir.AxisListType.X))
                    vv.wait(v)
                    vv.inc(v.tensor_scalar(st[:, 1:2], st[:, 0:1], 1.0 / D, None, op0=ALU.mult))
                    vv.wait(v)
                    vv.inc(v.tensor_scalar(xc[:, :], xt[:, :], st[:, 1:2], None, op0=ALU.subtract))
                @blk.scalar
                def _(sc, i=i):
                    vv.wait(sc)
                    vv.inc(sc.activation(sq[:, :], xc[:, :], AF.Square, accum_out=st[:, 2:3]))
                @blk.vector
                def _(v, i=i):
                    vv.wait(v)
                    vv.inc(v.tensor_scalar(st[:, 3:4], st[:, 2:3], 1.0 / D, EPS,
                                           op0=ALU.mult, op1=ALU.add))
                    vv.wait(v)
                    vv.inc(v.reciprocal(st[:, 4:5], st[:, 3:4]))
                @blk.scalar
                def _(sc, i=i):
                    vv.wait(sc)
                    vv.inc(sc.activation(st[:, 5:6], st[:, 4:5], AF.Sqrt))
                @blk.vector
                def _(v, i=i):
                    vv.wait(v)
                    vv.inc(v.tensor_scalar(xo[:, :], xc[:, :], st[:, 5:6], None, op0=ALU.mult))
                @blk.sync
                def _(s, i=i):
                    vv.wait(s)
                    so.inc(s.dma_start(xn[i * 128:(i + 1) * 128, :], xo[:, :]), 16)
                    so.wait(s)
    return nc



def build_mm_fm(Kdim, M, T, sign_weights=False, out_bf16=False):
    from contextlib import ExitStack
    assert M % 128 == 0 and Kdim % 128 == 0
    mt = M // 128
    kt = Kdim // 128
    tt = cdiv(T, 512)
    KF = kt * 128
    odt = BF16 if out_bf16 else F32
    nc = bass.Bass("TRN2", target_bir_lowering=False)
    xin = nc.dram_tensor("x", [Kdim, T], BF16, kind="ExternalInput").ap()
    w = nc.dram_tensor("w", [mt, Kdim, 128], F32, kind="ExternalInput").ap()
    out = nc.dram_tensor("o", [M, T], odt, kind="ExternalOutput").ap()
    if sign_weights:
        wsum = nc.dram_tensor("wsum", [128, mt], F32, kind="ExternalOutput").ap()

    with ExitStack() as es:
        xs = es.enter_context(nc.sbuf_tensor("xs", [128, kt * T], BF16))
        wa = es.enter_context(nc.sbuf_tensor("wa", [128, KF], BF16))
        wb = es.enter_context(nc.sbuf_tensor("wb", [128, KF], BF16))
        oa = es.enter_context(nc.sbuf_tensor("oa", [128, T], odt))
        ob = es.enter_context(nc.sbuf_tensor("ob", [128, T], odt))
        if sign_weights:
            sga = es.enter_context(nc.sbuf_tensor("sga", [128, KF], BF16))
            sgb = es.enter_context(nc.sbuf_tensor("sgb", [128, KF], BF16))
            accw = es.enter_context(nc.sbuf_tensor("accw", [128, mt], F32))
        ps = es.enter_context(nc.psum_tensor("ps", [128, 512], F32))
        ps2 = es.enter_context(nc.psum_tensor("ps2", [128, 512], F32))
        lx = Sem(es, nc, "lx")
        lw = Sem(es, nc, "lw")
        sgs = Sem(es, nc, "sg")
        tc = Sem(es, nc, "tc")
        mm = Sem(es, nc, "mm")
        ev = Sem(es, nc, "ev")
        so = Sem(es, nc, "so")

        with nc.Block() as blk:
            @blk.sync
            def _(s):
                for k in range(kt):
                    lx.inc(s.dma_start(xs[:, k * T:(k + 1) * T],
                                       xin[k * 128:(k + 1) * 128, :]), 16)

        wbufs = [wa, wb]
        sbufs = [sga, sgb] if sign_weights else [wa, wb]
        with nc.Block() as blk:
            @blk.gpsimd
            def _(g):
                for m in range(mt):
                    wf = wbufs[m % 2]
                    if m >= 2:
                        if sign_weights:
                            g.wait_ge(sgs.h, m - 1)
                        else:
                            g.wait_ge(mm.h, (m - 1) * tt)
                    g.dma_start(
                        wf[:, :].rearrange("p (k f) -> p k f", k=kt),
                        w[m, :, :].rearrange("(k p) f -> p k f", p=128),
                    ).then_inc(lw.h, 16)

            if sign_weights:
                @blk.scalar
                def _(sc):
                    for m in range(mt):
                        wf = wbufs[m % 2]
                        ws = sbufs[m % 2]
                        sc.wait_ge(lw.h, 16 * (m + 1))
                        if m >= 2:
                            sc.wait_ge(mm.h, (m - 1) * tt)
                        sc.activation(ws[:, :], wf[:, :], AF.Sign)
                        sc.activation(wf[:, :], wf[:, :], AF.Abs,
                                      accum_out=accw[:, m:m + 1]).then_inc(sgs.h, 1)

            @blk.tensor
            def _(t):
                cnt = 0
                for m in range(mt):
                    ws = sbufs[m % 2]
                    if sign_weights:
                        t.wait_ge(sgs.h, m + 1)
                    else:
                        t.wait_ge(lw.h, 16 * (m + 1))
                    if m == 0:
                        t.wait_ge(lx.h, 16 * kt)
                    for j in range(tt):
                        f = min(512, T - j * 512)
                        pst = ps if (m * tt + j) % 2 == 0 else ps2
                        if cnt >= 2:
                            t.wait_ge(ev.h, cnt - 1)
                        ins = None
                        for k in range(kt):
                            ins = t.matmul(pst[:, 0:f], ws[:, k * 128:(k + 1) * 128],
                                           xs[:, k * T + j * 512:k * T + j * 512 + f],
                                           start=(k == 0), stop=(k == kt - 1))
                        ins.then_inc(mm.h, 1)
                        cnt += 1

            @blk.vector
            def _(v):
                cnt = 0
                for m in range(mt):
                    ot = oa if m % 2 == 0 else ob
                    if m >= 2:
                        v.wait_ge(so.h, 16 * (m - 1))
                    for j in range(tt):
                        f = min(512, T - j * 512)
                        pst = ps if (m * tt + j) % 2 == 0 else ps2
                        cnt += 1
                        v.wait_ge(mm.h, cnt)
                        v.tensor_copy(ot[:, j * 512:j * 512 + f], pst[:, 0:f]).then_inc(ev.h, 1)

            @blk.sync
            def _(s):
                for m in range(mt):
                    ot = oa if m % 2 == 0 else ob
                    s.wait_ge(ev.h, m * tt + tt)
                    s.dma_start(out[m * 128:(m + 1) * 128, :], ot[:, :]).then_inc(so.h, 16)
                s.wait_ge(so.h, 16 * mt)

        if sign_weights:
            with nc.Block() as blk:
                @blk.sync
                def _(s):
                    s.dma_start(wsum, accw[:, :]).then_inc(so.h, 16)
                    s.wait_ge(so.h, 16 * (mt + 1))
    return nc



def build_conv_dt(T):
    from contextlib import ExitStack
    nc = bass.Bass("TRN2", target_bir_lowering=False)
    hbc = nc.dram_tensor("hbc", [CONV_DIM, T + 3], F32, kind="ExternalInput").ap()
    cw = nc.dram_tensor("cw", [CONV_DIM, 4], F32, kind="ExternalInput").ap()
    cb = nc.dram_tensor("cb", [CONV_DIM], F32, kind="ExternalInput").ap()
    dt = nc.dram_tensor("dt", [H, T], F32, kind="ExternalInput").ap()
    dtb = nc.dram_tensor("dtb", [H], F32, kind="ExternalInput").ap()
    xbc = nc.dram_tensor("xbc", [CONV_DIM, T], F32, kind="ExternalOutput").ap()
    dtp = nc.dram_tensor("dtp", [H, T], F32, kind="ExternalOutput").ap()
    nt = CONV_DIM // 128
    with ExitStack() as es:
        ht = es.enter_context(nc.sbuf_tensor("ht", [128, T + 3], F32))
        wt = es.enter_context(nc.sbuf_tensor("wt", [128, 4], F32))
        bt = es.enter_context(nc.sbuf_tensor("bt", [128, 1], F32))
        a0 = es.enter_context(nc.sbuf_tensor("a0", [128, T], F32))
        a1 = es.enter_context(nc.sbuf_tensor("a1", [128, T], F32))
        a2 = es.enter_context(nc.sbuf_tensor("a2", [128, T], F32))
        ld = Sem(es, nc, "ld")
        vv = Sem(es, nc, "vv")
        so = Sem(es, nc, "so")
        for i in range(nt):
            with nc.Block() as blk:
                @blk.sync
                def _(s, i=i):
                    r = slice(i * 128, (i + 1) * 128)
                    ld.inc(s.dma_start(ht[:, :], hbc[r, :]), 16)
                    ld.inc(s.dma_start(wt[:, :], cw[r, :]), 16)
                    ld.inc(s.dma_start(bt[:, :], cb[r].unsqueeze(1)), 16)
                @blk.vector
                def _(v, i=i):
                    ld.wait(v)
                    vv.inc(v.tensor_scalar(a0[:, :], ht[:, 3:3 + T], wt[:, 3:4], bt[:, :],
                                           op0=ALU.mult, op1=ALU.add))
                    for k in (2, 1, 0):
                        vv.wait(v)
                        src, dst = (a0, a1) if k % 2 == 0 else (a1, a0)
                        if k == 2:
                            src, dst = a0, a1
                        elif k == 1:
                            src, dst = a1, a0
                        else:
                            src, dst = a0, a1
                        vv.inc(v.scalar_tensor_tensor(dst[:, :], ht[:, k:k + T], wt[:, k:k + 1],
                                                      src[:, :], op0=ALU.mult, op1=ALU.add))
                @blk.scalar
                def _(sc, i=i):
                    vv.wait(sc)
                    vv.inc(sc.activation(a2[:, :], a1[:, :], AF.Silu))
                @blk.sync
                def _(s, i=i):
                    vv.wait(s)
                    so.inc(s.dma_start(xbc[i * 128:(i + 1) * 128, :], a2[:, :]), 16)
        with nc.Block() as blk:
            @blk.sync
            def _(s):
                ld.inc(s.dma_start(ht[0:H, 0:T], dt), 16)
                ld.inc(s.dma_start(bt[0:H, :], dtb.unsqueeze(1)), 16)
            @blk.vector
            def _(v):
                ld.wait(v)
                vv.inc(v.tensor_scalar(a0[0:H, :], ht[0:H, 0:T], bt[0:H, :], None, op0=ALU.add))
                vv.wait(v)
                vv.inc(v.tensor_scalar(a2[0:H, :], a0[0:H, :], 0.0, None, op0=ALU.max))
            @blk.scalar
            def _(sc):
                vv.wait(sc)
                vv.inc(sc.activation(a1[0:H, :], a0[0:H, :], AF.Abs))
                vv.wait(sc)
                vv.inc(sc.activation(a0[0:H, :], a1[0:H, :], AF.Exp, scale=-1.0))
                vv.wait(sc)
                vv.inc(sc.activation(a1[0:H, :], a0[0:H, :], AF.Ln, bias=1.0))
            @blk.vector
            def _(v):
                vv.wait(v)
                vv.inc(v.tensor_tensor(a0[0:H, :], a2[0:H, :], a1[0:H, :], op=ALU.add))
            @blk.sync
            def _(s):
                vv.wait(s)
                so.inc(s.dma_start(dtp, a0[0:H, :]), 16)
                so.wait(s)
    return nc



def build_scan(L=SEQ, NBH=16, NH=8, dbg_sub=None):
    from contextlib import ExitStack
    Q = 128
    NS = L // Q
    nc = bass.Bass("TRN2", target_bir_lowering=False)
    if dbg_sub is not None:
        dbg_mq = nc.dram_tensor("dbg_mq", [128, NBH * Q], F32, kind="ExternalOutput").ap()
        dbg_mf = nc.dram_tensor("dbg_mf", [128, NBH * Q], F32, kind="ExternalOutput").ap()
        dbg_ub = nc.dram_tensor("dbg_ub", [128, NBH * P], F32, kind="ExternalOutput").ap()
        dbg_pg = nc.dram_tensor("dbg_pg", [128, 256], F32, kind="ExternalOutput").ap()
    bfm_in = nc.dram_tensor("bfm", [2, N, L], F32, kind="ExternalInput").ap()
    cfm_in = nc.dram_tensor("cfm", [2, N, L], F32, kind="ExternalInput").ap()
    btm_in = nc.dram_tensor("btm", [2, L, N], F32, kind="ExternalInput").ap()
    xp_in = nc.dram_tensor("xp", [2, L, NH * P], F32, kind="ExternalInput").ap()
    dtph_in = nc.dram_tensor("dtph", [NBH, L], F32, kind="ExternalInput").ap()
    dtpt_in = nc.dram_tensor("dtpt", [L, NBH], F32, kind="ExternalInput").ap()
    alog_in = nc.dram_tensor("alog", [NBH], F32, kind="ExternalInput").ap()
    ident_in = nc.dram_tensor("identf", [128, 128], F32, kind="ExternalInput").ap()
    y2 = nc.dram_tensor("y2", [L, NBH * P], F32, kind="ExternalOutput").ap()
    lhm_dram = nc.dram_tensor("lhm", [NBH, L], F32, kind="Internal").ap()

    with ExitStack() as es:
        sbt = lambda nm, shp, dt_: es.enter_context(nc.sbuf_tensor(nm, shp, dt_))
        bfm = sbt("bfm_s", [128, 2 * L], BF16)
        cfm = sbt("cfm_s", [128, 2 * L], BF16)
        btm = sbt("btm_s", [128, 2 * NS * N], BF16)
        xps = sbt("xps_s", [128, 2 * NH * P], BF16)
        dtph = sbt("dtph_s", [NBH, L], F32)
        dtpc = sbt("dtpc_s", [128, NS * NBH], F32)
        alog = sbt("alog_s", [NBH, 1], F32)
        acol = sbt("acol_s", [NBH, 1], F32)
        identf = sbt("ident_s", [128, 128], F32)
        ldec = dtph
        lhm = sbt("lhm_s", [NBH, L], F32)
        rall = sbt("rall_s", [128, NBH * Q], F32)
        negltm = sbt("negltm_s", [128, NBH], F32)
        e3c = sbt("e3c_s", [128, NBH], F32)
        mp = sbt("mp_s", [128, NBH * Q], F32)
        mq = sbt("mq_s", [128, NBH * Q], F32)
        mf = sbt("mf_s", [128, NBH * Q], BF16)
        ub = sbt("ub_s", [128, NBH * P], BF16)
        utb = sbt("utb_s", [128, NBH * P], BF16)
        yb = sbt("yb_s", [128, NBH * P], F32)
        y1 = sbt("y1_s", [128, P], F32)
        sf0 = sbt("sf0_s", [128, NBH * P], F32)
        sf1 = sbt("sf1_s", [128, NBH * P], F32)
        sbf = sbt("sbf_s", [128, NBH * P], BF16)
        pt = es.enter_context(nc.psum_tensor("pt", [128, 512], F32))
        pg = es.enter_context(nc.psum_tensor("pg", [128, 512], F32))
        pyb = [es.enter_context(nc.psum_tensor(f"py{i}", [128, 512], F32)) for i in range(2)]
        pzb = [es.enter_context(nc.psum_tensor(f"pz{i}", [128, 512], F32)) for i in range(2)]
        pdb = [es.enter_context(nc.psum_tensor(f"pd{i}", [128, 512], F32)) for i in range(2)]
        ld = Sem(es, nc, "ld")
        vv = Sem(es, nc, "vv")
        ss = Sem(es, nc, "ss")
        tt_ = Sem(es, nc, "tt")
        so = Sem(es, nc, "so")
        pc = Sem(es, nc, "pc")

        with nc.Block() as blk:
            @blk.sync
            def _(s):
                ld.inc(s.dma_start(dtph[:, :], dtph_in), 16)
                ld.inc(s.dma_start(dtpc[:, :].rearrange("p (s c) -> p s c", s=NS),
                                   dtpt_in.rearrange("(s p) c -> p s c", p=128)), 16)
                ld.inc(s.dma_start(alog[:, :], alog_in.unsqueeze(1)), 16)
                ld.inc(s.dma_start(identf[:, :], ident_in), 16)
            @blk.gpsimd
            def _(g):
                for b in range(2):
                    ld.inc(g.dma_start(bfm[:, b * L:(b + 1) * L], bfm_in[b]), 16)
                    ld.inc(g.dma_start(cfm[:, b * L:(b + 1) * L], cfm_in[b]), 16)
                    ld.inc(g.dma_start(btm[:, b * NS * N:(b + 1) * NS * N]
                                       .rearrange("p (s n) -> p s n", s=NS),
                                       btm_in[b].rearrange("(s p) n -> p s n", p=128)), 16)
            @blk.vector
            def _(v):
                v.memset(sf0[:, :], 0.0)
                v.memset(sf1[:, :], 0.0)
                v.memset(sbf[:, :], 0.0)
        with nc.Block() as blk:
            @blk.scalar
            def _(sc):
                ld.wait(sc)
                vv.inc(sc.activation(acol[:, :], alog[:, :], AF.Exp))
            @blk.vector
            def _(v):
                ld.wait(v)
                vv.wait(v)
                vv.inc(v.tensor_scalar(acol[:, :], acol[:, :], -1.0, None, op0=ALU.mult))
                vv.wait(v)
                vv.inc(v.tensor_scalar(ldec[:, :], dtph[:, :], acol[:, :], None, op0=ALU.mult))
                vv.wait(v)
                for s_ in range(NS):
                    vv.inc(v.tensor_tensor_scan(lhm[:, s_ * Q:(s_ + 1) * Q],
                                                ldec[:, s_ * Q:(s_ + 1) * Q],
                                                ldec[:, s_ * Q:(s_ + 1) * Q],
                                                initial=0.0, op0=ALU.add, op1=ALU.bypass))
            @blk.sync
            def _(s):
                vv.wait(s)
                so.inc(s.dma_start(lhm_dram, lhm[:, :]), 16)

        for s_ in range(NS):
            sl = slice(s_ * Q, (s_ + 1) * Q)
            par = s_ % 2
            scur = sf0 if par == 0 else sf1
            snew = sf1 if par == 0 else sf0
            with nc.Block() as blk:
                @blk.sync
                def _(s, s_=s_, sl=sl):
                    so.wait(s)
                    so.inc(s.dma_start(
                        rall[:, :].rearrange("p (h t) -> p h t", h=NBH),
                        lhm_dram[:, sl].partition_broadcast(128)), 16)
                @blk.gpsimd
                def _(g, s_=s_):
                    for b in range(2):
                        so.inc(g.dma_start(xps[:, b * NH * P:(b + 1) * NH * P],
                                           xp_in[b, s_ * Q:(s_ + 1) * Q, :]), 16)
                @blk.tensor
                def _(t, s_=s_, sl=sl):
                    if s_ == 0:
                        ld.wait(t)
                        vv.wait(t)
                    tt_.inc(t.transpose(pt[:, 0:NBH], lhm[0:NBH, sl], identf[0:NBH, 0:NBH]))
                    for b in range(2):
                        tt_.inc(t.matmul(pg[:, b * 128:(b + 1) * 128],
                                         bfm[:, b * L + s_ * Q:b * L + (s_ + 1) * Q],
                                         cfm[:, b * L + s_ * Q:b * L + (s_ + 1) * Q],
                                         start=True, stop=True))
                @blk.vector
                def _(v, s_=s_):
                    if s_ == 0:
                        ld.wait(v)
                    tt_.wait(v, tt_.v - 2)
                    vv.inc(v.tensor_scalar(negltm[:, :], pt[:, 0:NBH], -1.0, None, op0=ALU.mult))
                    so.wait(v)
                    for bh in range(NBH):
                        b, hl = bh // NH, bh % NH
                        vv.inc(v.tensor_scalar(
                            ub[:, bh * P:(bh + 1) * P],
                            xps[:, (b * NH + hl) * P:(b * NH + hl + 1) * P],
                            dtpc[:, s_ * NBH + bh:s_ * NBH + bh + 1], None, op0=ALU.mult))
                @blk.scalar
                def _(sc, s_=s_):
                    vv.wait(sc, vv.v - NBH)
                    so.wait(sc)
                    for bh in range(NBH):
                        vv.inc(sc.activation(mp[:, bh * Q:(bh + 1) * Q], rall[:, bh * Q:(bh + 1) * Q],
                                             AF.Exp, bias=negltm[:, bh:bh + 1]))
                    vv.inc(sc.activation(e3c[:, :], negltm[:, :], AF.Exp, scale=-1.0))
            with nc.Block() as blk:
                @blk.gpsimd
                def _(g, s_=s_):
                    ss.inc(g.affine_select(mq[:, :], mp[:, :],
                                           fill=0.0, channel_multiplier=-1, base=0,
                                           pattern=[[0, NBH], [1, Q]], compare_op=ALU.is_ge))
                @blk.vector
                def _(v, s_=s_):
                    ss.wait(v)
                    tt_.wait(v)
                    for bh in range(NBH):
                        b = bh // NH
                        vv.inc(v.tensor_tensor(mf[:, bh * Q:(bh + 1) * Q],
                                               mq[:, bh * Q:(bh + 1) * Q],
                                               pg[:, b * 128:(b + 1) * 128], op=ALU.mult))
                        vv.inc(v.tensor_scalar(utb[:, bh * P:(bh + 1) * P],
                                               ub[:, bh * P:(bh + 1) * P],
                                               mq[:, bh * Q + Q - 1:bh * Q + Q], None,
                                               op0=ALU.mult))
            if dbg_sub is not None and s_ == dbg_sub:
                with nc.Block() as blk:
                    @blk.vector
                    def _(v):
                        vv.inc(v.tensor_copy(mp[:, :], mf[:, :]))
                        vv.inc(v.tensor_copy(rall[:, 0:256], pg[:, 0:256]))
                    @blk.sync
                    def _(s):
                        vv.wait(s)
                        so.inc(s.dma_start(dbg_mq, mq[:, :]), 16)
                        so.inc(s.dma_start(dbg_mf, mp[:, :]), 16)
                        so.inc(s.dma_start(dbg_pg, rall[:, 0:256]), 16)
                    @blk.gpsimd
                    def _(g):
                        so.wait(g)
                        so.inc(g.dma_start(dbg_ub, ub[:, :]), 16)
            with nc.Block() as blk:
                pc0 = pc.v
                @blk.tensor
                def _(t, s_=s_, pc0=pc0):
                    vv.wait(t)
                    ss.wait(t)
                    for bh in range(NBH):
                        b = bh // NH
                        pp = bh % 2
                        if bh >= 2:
                            t.wait_ge(pc.h, pc0 + bh - 1)
                        tt_.inc(t.matmul(pyb[pp][:, 0:P], mf[:, bh * Q:(bh + 1) * Q],
                                         ub[:, bh * P:(bh + 1) * P], start=True, stop=True))
                        tt_.inc(t.matmul(pzb[pp][:, 0:P],
                                         cfm[:, b * L + s_ * Q:b * L + (s_ + 1) * Q],
                                         sbf[:, bh * P:(bh + 1) * P], start=True, stop=True))
                        tt_.inc(t.matmul(pdb[pp][:, 0:P],
                                         btm[:, (b * NS + s_) * N:(b * NS + s_ + 1) * N],
                                         utb[:, bh * P:(bh + 1) * P], start=True, stop=True))
                @blk.vector
                def _(v, s_=s_, scur=scur, snew=snew):
                    base_tt = tt_.v - 3 * NBH
                    for bh in range(NBH):
                        pp = bh % 2
                        sly = slice(bh * P, (bh + 1) * P)
                        v.wait_ge(tt_.h, base_tt + 3 * bh + 3)
                        vv.inc(v.tensor_copy(y1[:, :], pyb[pp][:, 0:P]))
                        vv.wait(v)
                        vv.inc(v.scalar_tensor_tensor(yb[:, sly], pzb[pp][:, 0:P],
                                                      e3c[:, bh:bh + 1], y1[:, :],
                                                      op0=ALU.mult, op1=ALU.add))
                        vv.wait(v)
                        vv.inc(v.tensor_tensor(y1[:, 0:1], mq[:, bh * Q + Q - 1:bh * Q + Q],
                                               e3c[:, bh:bh + 1], op=ALU.mult))
                        vv.wait(v)
                        pc.inc(v.scalar_tensor_tensor(snew[:, sly], scur[:, sly],
                                                      y1[:, 0:1], pdb[pp][:, 0:P],
                                                      op0=ALU.mult, op1=ALU.add))
                        pc.wait(v)
                        vv.inc(v.tensor_copy(sbf[:, sly], snew[:, sly]))
                @blk.sync
                def _(s, s_=s_):
                    vv.wait(s)
                    so.inc(s.dma_start(y2[s_ * Q:(s_ + 1) * Q, :], yb[:, :]), 16)
    return nc



def build_gnorm_gate(T):
    from contextlib import ExitStack
    nc = bass.Bass("TRN2", target_bir_lowering=False)
    y_in = nc.dram_tensor("y", [T, D], F32, kind="ExternalInput").ap()
    xp_in = nc.dram_tensor("xp", [T, D], F32, kind="ExternalInput").ap()
    gate_in = nc.dram_tensor("gate", [T, D], F32, kind="ExternalInput").ap()
    drep_in = nc.dram_tensor("drep", [D], F32, kind="ExternalInput").ap()
    nw_in = nc.dram_tensor("nw", [D], F32, kind="ExternalInput").ap()
    yn_out = nc.dram_tensor("yn", [T, D], BF16, kind="ExternalOutput").ap()
    nt = T // 128
    GS = D // G
    with ExitStack() as es:
        sbt = lambda nm, shp, dt_: es.enter_context(nc.sbuf_tensor(nm, shp, dt_))
        yt = sbt("yt", [128, D], F32)
        xt = sbt("xt", [128, D], F32)
        gt = sbt("gt", [128, D], F32)
        dr = sbt("dr", [128, D], F32)
        nw = sbt("nw_s", [128, D], F32)
        t1 = sbt("t1", [128, D], F32)
        sq = sbt("sq", [128, D], F32)
        gs = sbt("gs", [128, 32], F32)
        on = sbt("on", [128, D], BF16)
        ld = Sem(es, nc, "ld")
        vv = Sem(es, nc, "vv")
        so = Sem(es, nc, "so")
        with nc.Block() as blk:
            @blk.sync
            def _(s):
                ld.inc(s.dma_start(dr[:, :], drep_in.unsqueeze(0).broadcast_to((128, D))), 16)
                ld.inc(s.dma_start(nw[:, :], nw_in.unsqueeze(0).broadcast_to((128, D))), 16)
        for i in range(nt):
            r = slice(i * 128, (i + 1) * 128)
            with nc.Block() as blk:
                @blk.sync
                def _(s, r=r):
                    ld.inc(s.dma_start(yt[:, :], y_in[r, :]), 16)
                    ld.inc(s.dma_start(xt[:, :], xp_in[r, :]), 16)
                    ld.inc(s.dma_start(gt[:, :], gate_in[r, :]), 16)
                @blk.vector
                def _(v, i=i):
                    ld.wait(v)
                    vv.inc(v.tensor_tensor(t1[:, :], xt[:, :], dr[:, :], op=ALU.mult))
                    vv.wait(v)
                    vv.inc(v.tensor_tensor(yt[:, :], yt[:, :], t1[:, :], op=ALU.add))
                @blk.scalar
                def _(sc, i=i):
                    vv.wait(sc)
                    vv.inc(sc.activation(sq[:, :], yt[:, :], AF.Square))
                    ld.wait(sc)
                    vv.inc(sc.activation(gt[:, :], gt[:, :], AF.Silu))
                @blk.vector
                def _(v, i=i):
                    vv.wait(v)
                    for g in range(G):
                        vv.inc(v.reduce_sum(gs[:, g:g + 1], sq[:, g * GS:(g + 1) * GS],
                                            axis=mybir.AxisListType.X))
                    vv.wait(v)
                    vv.inc(v.tensor_scalar(gs[:, 8:16], gs[:, 0:8], 1.0 / GS, EPS,
                                           op0=ALU.mult, op1=ALU.add))
                    vv.wait(v)
                    vv.inc(v.reciprocal(gs[:, 16:24], gs[:, 8:16]))
                @blk.scalar
                def _(sc, i=i):
                    vv.wait(sc)
                    vv.inc(sc.activation(gs[:, 24:32], gs[:, 16:24], AF.Sqrt))
                @blk.vector
                def _(v, i=i):
                    vv.wait(v)
                    for g in range(G):
                        vv.inc(v.tensor_scalar(t1[:, g * GS:(g + 1) * GS],
                                               yt[:, g * GS:(g + 1) * GS],
                                               gs[:, 24 + g:25 + g], None, op0=ALU.mult))
                    vv.wait(v)
                    vv.inc(v.tensor_tensor(sq[:, :], t1[:, :], gt[:, :], op=ALU.mult))
                    vv.wait(v)
                    vv.inc(v.tensor_tensor(on[:, :], sq[:, :], nw[:, :], op=ALU.mult))
                @blk.sync
                def _(s, r=r):
                    vv.wait(s)
                    so.inc(s.dma_start(yn_out[r, :], on[:, :]), 16)
        with nc.Block() as blk:
            @blk.sync
            def _(s):
                so.wait(s)
    return nc



def build_l0_tail(T):
    from contextlib import ExitStack
    mt = D // 128
    kt = D // 128
    tt = cdiv(T, 512)
    nc = bass.Bass("TRN2", target_bir_lowering=False)
    yn_in = nc.dram_tensor("yn", [D, T], BF16, kind="ExternalInput").ap()
    xn_in = nc.dram_tensor("xn", [D, T], BF16, kind="ExternalInput").ap()
    x_in = nc.dram_tensor("x", [D, T], F32, kind="ExternalInput").ap()
    wo_in = nc.dram_tensor("wo", [mt, D, 128], F32, kind="ExternalInput").ap()
    wf_in = nc.dram_tensor("wf", [mt, D, 128], F32, kind="ExternalInput").ap()
    x1_out = nc.dram_tensor("x1", [D, T], F32, kind="ExternalOutput").ap()
    KF = kt * 128
    with ExitStack() as es:
        sbt = lambda nm, shp, dt_: es.enter_context(nc.sbuf_tensor(nm, shp, dt_))
        ys = sbt("ys_", [128, kt * T], BF16)
        xs = sbt("xs_", [128, kt * T], BF16)
        wa = sbt("wa_", [128, KF], BF16)
        wb = sbt("wb_", [128, KF], BF16)
        sga = sbt("sga_", [128, KF], BF16)
        sgb = sbt("sgb_", [128, KF], BF16)
        accw = sbt("accw_", [128, mt + 8], F32)
        sst = sbt("sst_", [128, T], F32)
        gst = sbt("gst_", [128, T], F32)
        xrt = sbt("xrt_", [128, T], F32)
        xot = sbt("xot_", [128, T], F32)
        silt = sbt("silt_", [128, T], F32)
        scl = sbt("scl_", [128, 1], F32)
        ones_r = sbt("ones_r", [1, 128], F32)
        ps = es.enter_context(nc.psum_tensor("ps", [128, 512], F32))
        ps2 = es.enter_context(nc.psum_tensor("ps2", [128, 512], F32))
        lx = Sem(es, nc, "lx")
        lw = Sem(es, nc, "lw")
        sgs = Sem(es, nc, "sg")
        mm = Sem(es, nc, "mm")
        ev = Sem(es, nc, "ev")
        so = Sem(es, nc, "so")
        vv = Sem(es, nc, "vv")

        with nc.Block() as blk:
            @blk.gpsimd
            def _(g):
                for m in range(mt):
                    wf = [wa, wb][m % 2]
                    if m >= 2:
                        g.wait_ge(sgs.h, m - 1)
                    lw.inc(g.dma_start(wf[:, :].rearrange("p (k f) -> p k f", k=kt),
                                wf_in[m, :, :].rearrange("(k p) f -> p k f", p=128)
                                ), 16)
            @blk.scalar
            def _(sc):
                for m in range(mt):
                    wf = [wa, wb][m % 2]
                    sc.wait_ge(lw.h, 16 * (m + 1))
                    sc.activation(sga[:, :], wf[:, :], AF.Abs,
                                  accum_out=accw[:, m:m + 1]).then_inc(sgs.h, 1)
            @blk.sync
            def _(s):
                for k in range(kt):
                    lx.inc(s.dma_start(ys[:, k * T:(k + 1) * T], yn_in[k * 128:(k + 1) * 128, :]), 16)
                    lx.inc(s.dma_start(xs[:, k * T:(k + 1) * T], xn_in[k * 128:(k + 1) * 128, :]), 16)
        with nc.Block() as blk:
            @blk.vector
            def _(v):
                sgs.wait(v)
                vv.inc(v.reduce_sum(accw[:, mt:mt + 1], accw[:, 0:mt], axis=mybir.AxisListType.X))
                vv.wait(v)
                vv.inc(v.memset(ones_r[:, :], 1.0))
                vv.wait(v)
                vv.inc(v.memset(accw[:, mt + 2:mt + 3], 1.0))
            @blk.tensor
            def _(t):
                vv.wait(t)
                vv.inc(t.matmul(ps[0:1, 0:1], accw[:, mt + 2:mt + 3], accw[:, mt:mt + 1],
                                start=True, stop=True))
            @blk.vector
            def _(v):
                vv.wait(v)
                vv.inc(v.tensor_copy(accw[0:1, mt + 1:mt + 2], ps[0:1, 0:1]))
            @blk.tensor
            def _(t):
                vv.wait(t)
                vv.inc(t.matmul(ps2[:, 0:1], ones_r[:, :], accw[0:1, mt + 1:mt + 2],
                                start=True, stop=True))
            @blk.vector
            def _(v):
                vv.wait(v)
                vv.inc(v.tensor_scalar(scl[:, :], ps2[:, 0:1], 1.0 / (D * D), None, op0=ALU.mult))

        wbufs = [wa, wb]
        sbufs = [sga, sgb]
        for m in range(mt):
            r = slice(m * 128, (m + 1) * 128)
            with nc.Block() as blk:
                @blk.gpsimd
                def _(g, m=m):
                    lw.inc(g.dma_start(wa[:, :].rearrange("p (k f) -> p k f", k=kt),
                                wo_in[m, :, :].rearrange("(k p) f -> p k f", p=128)), 16)
                    lw.inc(g.dma_start(wb[:, :].rearrange("p (k f) -> p k f", k=kt),
                                wf_in[m, :, :].rearrange("(k p) f -> p k f", p=128)), 16)
                @blk.sync
                def _(s, r=r):
                    so.inc(s.dma_start(xrt[:, :], x_in[r, :]), 16)
                @blk.scalar
                def _(sc, m=m):
                    lw.wait(sc)
                    vv.inc(sc.activation(sgb[:, :], wb[:, :], AF.Sign))
                @blk.tensor
                def _(t, m=m, ev0=None):
                    if m == 0:
                        t.wait_ge(lx.h, 16 * 2 * kt)
                    lw.wait(t)
                    vv.wait(t)
                    ev_base = ev.v
                    for j in range(tt):
                        f = min(512, T - j * 512)
                        if j >= 1:
                            t.wait_ge(ev.h, ev_base + 2 * j)
                        ins = None
                        for k in range(kt):
                            ins = t.matmul(ps[:, 0:f], wa[:, k * 128:(k + 1) * 128],
                                           ys[:, k * T + j * 512:k * T + j * 512 + f],
                                           start=(k == 0), stop=(k == kt - 1))
                        mm.inc(ins)
                        ins = None
                        for k in range(kt):
                            ins = t.matmul(ps2[:, 0:f], sgb[:, k * 128:(k + 1) * 128],
                                           xs[:, k * T + j * 512:k * T + j * 512 + f],
                                           start=(k == 0), stop=(k == kt - 1))
                        mm.inc(ins)
                @blk.vector
                def _(v, m=m):
                    base = mm.v - 2 * tt
                    for j in range(tt):
                        f = min(512, T - j * 512)
                        v.wait_ge(mm.h, base + 2 * j + 2)
                        ev.inc(v.tensor_copy(sst[:, j * 512:j * 512 + f], ps[:, 0:f]))
                        ev.inc(v.tensor_scalar(gst[:, j * 512:j * 512 + f], ps2[:, 0:f],
                                               scl[:, :], None, op0=ALU.mult))
                @blk.scalar
                def _(sc, m=m):
                    ev.wait(sc)
                    vv.inc(sc.activation(silt[:, 0:T], gst[:, :], AF.Silu))
                @blk.vector
                def _(v, m=m):
                    vv.wait(v)
                    so.wait(v)
                    vv.inc(v.tensor_tensor(gst[:, :], sst[:, :], silt[:, 0:T], op=ALU.mult))
                    vv.wait(v)
                    vv.inc(v.tensor_tensor(xot[:, :], xrt[:, :], gst[:, :], op=ALU.add))
                @blk.sync
                def _(s, r=r, m=m):
                    vv.wait(s)
                    so.inc(s.dma_start(x1_out[r, :], xot[:, :]), 16)
        with nc.Block() as blk:
            @blk.sync
            def _(s):
                so.wait(s)
    return nc



PI = float(np.pi)


def build_qkv(T):
    from contextlib import ExitStack
    kt = D // 128
    tt = cdiv(T, 512)
    nc = bass.Bass("TRN2", target_bir_lowering=False)
    xn_in = nc.dram_tensor("xn", [D, T], BF16, kind="ExternalInput").ap()
    wq_in = nc.dram_tensor("wq", [32, D, 128], F32, kind="ExternalInput").ap()
    wk_in = nc.dram_tensor("wk", [32, D, 128], F32, kind="ExternalInput").ap()
    wv_in = nc.dram_tensor("wv", [8, D, 512], F32, kind="ExternalInput").ap()
    pos_in = nc.dram_tensor("pos", [T], F32, kind="ExternalInput").ap()
    qr_out = nc.dram_tensor("qr", [D, T], BF16, kind="ExternalOutput").ap()
    kr_out = nc.dram_tensor("kr", [D, T], BF16, kind="ExternalOutput").ap()
    vt_out = nc.dram_tensor("vt", [T, D], BF16, kind="ExternalOutput").ap()
    csd = nc.dram_tensor("csd", [2, T], F32, kind="Internal").ap()
    with ExitStack() as es:
        sbt = lambda nm, shp, dt_: es.enter_context(nc.sbuf_tensor(nm, shp, dt_))
        xs = sbt("xs_", [128, kt * T], BF16)
        wva = sbt("wva_", [128, kt * 512], BF16)
        wvb = sbt("wvb_", [128, kt * 512], BF16)
        wa = sbt("wa_", [128, kt * 128], BF16)
        wb = sbt("wb_", [128, kt * 128], BF16)
        rp = sbt("rp_", [1, 8 * T], F32)
        rpi = sbt("rpi_", [1, T], mybir.dt.int32)
        cosb = sbt("cosb_", [64, T], F32)
        sinb = sbt("sinb_", [64, T], F32)
        tm1 = sbt("tm1_", [64, T], F32)
        tm2 = sbt("tm2_", [64, T], F32)
        oute = sbt("oute_", [64, T], BF16)
        outo = sbt("outo_", [64, T], BF16)
        vst = sbt("vst_", [128, 512], BF16)
        ps = es.enter_context(nc.psum_tensor("ps", [128, 512], F32))
        ps2 = es.enter_context(nc.psum_tensor("ps2", [128, 512], F32))
        pe1 = es.enter_context(nc.psum_tensor("pe1", [64, 512], F32))
        po1 = es.enter_context(nc.psum_tensor("po1", [64, 512], F32))
        pe2 = es.enter_context(nc.psum_tensor("pe2", [64, 512], F32))
        po2 = es.enter_context(nc.psum_tensor("po2", [64, 512], F32))
        lx = Sem(es, nc, "lx")
        lw = Sem(es, nc, "lw")
        mm = Sem(es, nc, "mm")
        ev = Sem(es, nc, "ev")
        so = Sem(es, nc, "so")
        vv = Sem(es, nc, "vv")

        with nc.Block() as blk:
            @blk.sync
            def _(s):
                for k in range(kt):
                    lx.inc(s.dma_start(xs[:, k * T:(k + 1) * T], xn_in[k * 128:(k + 1) * 128, :]), 16)
                lx.inc(s.dma_start(rp[0:1, 0:T], pos_in.unsqueeze(0)), 16)
        with nc.Block() as blk:
            @blk.vector
            def _(v):
                lx.wait(v)
                R = lambda a, b: rp[0:1, a * T:a * T + T] if b is None else None
                vv.inc(v.tensor_scalar(rp[0:1, T:2 * T], rp[0:1, 0:T], 1.0 / (2 * PI), None, op0=ALU.mult))
                vv.wait(v)
                vv.inc(v.tensor_copy(rpi[0:1, :], rp[0:1, T:2 * T]))
                vv.wait(v)
                vv.inc(v.tensor_copy(rp[0:1, 2 * T:3 * T], rpi[0:1, :]))
                vv.wait(v)
                vv.inc(v.tensor_tensor(rp[0:1, 3 * T:4 * T], rp[0:1, T:2 * T],
                                       rp[0:1, 2 * T:3 * T], op=ALU.subtract))
                vv.wait(v)
                vv.inc(v.tensor_scalar(rp[0:1, 4 * T:5 * T], rp[0:1, 3 * T:4 * T], 0.5, None, op0=ALU.is_gt))
                vv.wait(v)
                vv.inc(v.scalar_tensor_tensor(rp[0:1, 5 * T:6 * T], rp[0:1, 4 * T:5 * T], -1.0,
                                              rp[0:1, 3 * T:4 * T], op0=ALU.mult, op1=ALU.add))
                vv.wait(v)
                vv.inc(v.tensor_scalar(rp[0:1, 4 * T:5 * T], rp[0:1, 5 * T:6 * T], -0.5, None, op0=ALU.is_lt))
                vv.wait(v)
                vv.inc(v.scalar_tensor_tensor(rp[0:1, 3 * T:4 * T], rp[0:1, 4 * T:5 * T], 1.0,
                                              rp[0:1, 5 * T:6 * T], op0=ALU.mult, op1=ALU.add))
                vv.wait(v)
                vv.inc(v.tensor_scalar(rp[0:1, 2 * T:3 * T], rp[0:1, 3 * T:4 * T], 2 * PI, None, op0=ALU.mult))
            @blk.scalar
            def _(sc):
                vv.wait(sc)
                vv.inc(sc.activation(rp[0:1, 6 * T:7 * T], rp[0:1, 2 * T:3 * T], AF.Sin))
            @blk.vector
            def _(v):
                vv.wait(v)
                vv.inc(v.tensor_scalar(rp[0:1, 4 * T:5 * T], rp[0:1, 2 * T:3 * T], PI / 2, None, op0=ALU.add))
                vv.wait(v)
                vv.inc(v.tensor_scalar(rp[0:1, 5 * T:6 * T], rp[0:1, 2 * T:3 * T], PI / 2, None, op0=ALU.is_gt))
                vv.wait(v)
                vv.inc(v.scalar_tensor_tensor(rp[0:1, 3 * T:4 * T], rp[0:1, 5 * T:6 * T], -2 * PI,
                                              rp[0:1, 4 * T:5 * T], op0=ALU.mult, op1=ALU.add))
            @blk.scalar
            def _(sc):
                vv.wait(sc)
                vv.inc(sc.activation(rp[0:1, 7 * T:8 * T], rp[0:1, 3 * T:4 * T], AF.Sin))
            @blk.sync
            def _(s):
                vv.wait(s)
                so.inc(s.dma_start(csd[0:1, :], rp[0:1, 7 * T:8 * T]), 16)
                so.inc(s.dma_start(csd[1:2, :], rp[0:1, 6 * T:7 * T]), 16)
        with nc.Block() as blk:
            @blk.sync
            def _(s):
                so.wait(s)
                so.inc(s.dma_start(cosb[:, :], csd[0:1, :].broadcast_to((64, T))), 16)
                so.inc(s.dma_start(sinb[:, :], csd[1:2, :].broadcast_to((64, T))), 16)

        for which, w_in, dst in [(0, wq_in, qr_out), (1, wk_in, kr_out)]:
            for m in range(32):
                with nc.Block() as blk:
                    @blk.gpsimd
                    def _(g, m=m, w_in=w_in):
                        wf = [wa, wb][m % 2]
                        lw.inc(g.dma_start(wf[:, :].rearrange("p (k f) -> p k f", k=kt),
                                           w_in[m, :, :].rearrange("(k p) f -> p k f", p=128)), 16)
                    @blk.tensor
                    def _(t, m=m):
                        if m == 0:
                            lx.wait(t)
                        lw.wait(t)
                        for j in range(tt):
                            f = min(512, T - j * 512)
                            pse = [pe1, pe2][j % 2]
                            pso = [po1, po2][j % 2]
                            wf = [wa, wb][m % 2]
                            ins = None
                            for k in range(kt):
                                t.matmul(pse[:, 0:f], wf[:, k * 128:k * 128 + 64],
                                         xs[:, k * T + j * 512:k * T + j * 512 + f],
                                         start=(k == 0), stop=(k == kt - 1))
                                ins = t.matmul(pso[:, 0:f], wf[:, k * 128 + 64:(k + 1) * 128],
                                               xs[:, k * T + j * 512:k * T + j * 512 + f],
                                               start=(k == 0), stop=(k == kt - 1))
                            mm.inc(ins)
                    @blk.vector
                    def _(v, m=m):
                        base = mm.v - tt
                        so.wait(v)
                        for j in range(tt):
                            f = min(512, T - j * 512)
                            pse = [pe1, pe2][j % 2]
                            pso = [po1, po2][j % 2]
                            sl = slice(j * 512, j * 512 + f)
                            v.wait_ge(mm.h, base + j + 1)
                            vv.inc(v.tensor_tensor(tm1[:, sl], pse[:, 0:f], cosb[:, sl], op=ALU.mult))
                            vv.inc(v.tensor_tensor(tm2[:, sl], pso[:, 0:f], sinb[:, sl], op=ALU.mult))
                            vv.wait(v)
                            vv.inc(v.tensor_tensor(oute[:, sl], tm1[:, sl], tm2[:, sl], op=ALU.subtract))
                            vv.inc(v.tensor_tensor(tm1[:, sl], pse[:, 0:f], sinb[:, sl], op=ALU.mult))
                            vv.inc(v.tensor_tensor(tm2[:, sl], pso[:, 0:f], cosb[:, sl], op=ALU.mult))
                            vv.wait(v)
                            vv.inc(v.tensor_tensor(outo[:, sl], tm1[:, sl], tm2[:, sl], op=ALU.add))
                    @blk.sync
                    def _(s, m=m, dst=dst):
                        vv.wait(s)
                        so.inc(s.dma_start(dst[m * 128:m * 128 + 64, :], oute[:, :]), 16)
                        so.inc(s.dma_start(dst[m * 128 + 64:(m + 1) * 128, :], outo[:, :]), 16)

        for ob in range(8):
            with nc.Block() as blk:
                @blk.gpsimd
                def _(g, ob=ob):
                    wf = [wva, wvb][ob % 2]
                    lw.inc(g.dma_start(wf[:, :].rearrange("p (k f) -> p k f", k=kt),
                                       wv_in[ob, :, :].rearrange("(k p) f -> p k f", p=128)), 16)
                @blk.tensor
                def _(t, ob=ob):
                    lw.wait(t)
                    wf = [wva, wvb][ob % 2]
                    for tb in range(T // 128):
                        pst = [ps, ps2][tb % 2]
                        if tb >= 2:
                            t.wait_ge(ev.h, ev.v + tb - 1)
                        ins = None
                        for k in range(kt):
                            ins = t.matmul(pst[:, :], xs[:, k * T + tb * 128:k * T + (tb + 1) * 128],
                                           wf[:, k * 512:(k + 1) * 512],
                                           start=(k == 0), stop=(k == kt - 1))
                        mm.inc(ins)
                @blk.vector
                def _(v, ob=ob):
                    base = mm.v - T // 128
                    for tb in range(T // 128):
                        pst = [ps, ps2][tb % 2]
                        v.wait_ge(mm.h, base + tb + 1)
                        if tb >= 1:
                            v.wait_ge(so.h, so.v + 16 * (tb - 1))
                        ev.inc(v.tensor_copy(vst[:, :], pst[:, :]))
                @blk.sync
                def _(s, ob=ob):
                    base = ev.v - T // 128
                    for tb in range(T // 128):
                        s.wait_ge(ev.h, base + tb + 1)
                        so.inc(s.dma_start(vt_out[tb * 128:(tb + 1) * 128, ob * 512:(ob + 1) * 512],
                                           vst[:, :]), 16)
    return nc



def build_attn(L=SEQ, NBH=8):
    from contextlib import ExitStack
    HD = 128
    QC = 512
    nq = L // QC
    NT = L // 128
    SC = 1.0 / float(np.sqrt(HD))
    nc = bass.Bass("TRN2", target_bir_lowering=False)
    q_in = nc.dram_tensor("q", [NBH, HD, L], BF16, kind="ExternalInput").ap()
    k_in = nc.dram_tensor("k", [NBH, HD, L], BF16, kind="ExternalInput").ap()
    v_in = nc.dram_tensor("v", [NBH, L, HD], BF16, kind="ExternalInput").ap()
    o_out = nc.dram_tensor("o", [NBH, L, HD], BF16, kind="ExternalOutput").ap()
    with ExitStack() as es:
        sbt = lambda nm, shp, dt_: es.enter_context(nc.sbuf_tensor(nm, shp, dt_))
        qs = sbt("qs_", [128, 2 * L], BF16)
        ks = sbt("ks_", [128, 2 * L], BF16)
        vs = sbt("vs_", [128, 2 * NT * HD], BF16)
        q2 = sbt("q2_", [128, L], BF16)
        nrm = sbt("nrm_", [1, 4 * L + 8], F32)
        negm = sbt("negm_", [1, L], BF16)
        onesr = sbt("onesr_", [1, 128], BF16)
        onesc = sbt("onesc_", [128, 1], BF16)
        pbuf = sbt("pbuf_", [128, NT * QC], BF16)
        rcp = sbt("rcp_", [128, 4], F32)
        ost = sbt("ost_", [128, QC], BF16)
        psA = es.enter_context(nc.psum_tensor("psA", [128, 512], F32))
        psB = es.enter_context(nc.psum_tensor("psB", [128, 512], F32))
        psO = es.enter_context(nc.psum_tensor("psO", [128, 512], F32))
        psD = es.enter_context(nc.psum_tensor("psD", [128, 512], F32))
        psN = es.enter_context(nc.psum_tensor("psN", [128, 512], F32))
        psN2 = es.enter_context(nc.psum_tensor("psN2", [128, 512], F32))
        ld = Sem(es, nc, "ld")
        mm = Sem(es, nc, "mm")
        ex = Sem(es, nc, "ex")
        af = Sem(es, nc, "af")
        so = Sem(es, nc, "so")
        vv = Sem(es, nc, "vv")

        with nc.Block() as blk:
            @blk.vector
            def _(v):
                vv.inc(v.memset(onesr[:, :], 1.0))
                vv.inc(v.memset(onesc[:, :], 1.0))
            @blk.sync
            def _(s):
                ld.inc(s.dma_start(qs[:, 0:L], q_in[0]), 16)
                ld.inc(s.dma_start(ks[:, 0:L], k_in[0]), 16)
                ld.inc(s.dma_start(vs[:, 0:NT * HD].rearrange("p (t c) -> p t c", t=NT),
                                   v_in[0].rearrange("(t p) c -> p t c", p=128)), 16)

        for bh in range(NBH):
            par = bh % 2
            qb = qs[:, par * L:(par + 1) * L]
            kb = ks[:, par * L:(par + 1) * L]
            vb_off = par * NT * HD
            with nc.Block() as blk:
                @blk.sync
                def _(s, bh=bh):
                    if bh + 1 < NBH:
                        np_ = (bh + 1) % 2
                        ld.inc(s.dma_start(qs[:, np_ * L:(np_ + 1) * L], q_in[bh + 1]), 16)
                        ld.inc(s.dma_start(ks[:, np_ * L:(np_ + 1) * L], k_in[bh + 1]), 16)
                        ld.inc(s.dma_start(
                            vs[:, np_ * NT * HD:(np_ + 1) * NT * HD]
                            .rearrange("p (t c) -> p t c", t=NT),
                            v_in[bh + 1].rearrange("(t p) c -> p t c", p=128)), 16)
                for which in range(2):
                    @blk.scalar
                    def _(sc, bh=bh, which=which, qb=qb, kb=kb):
                        if which == 0:
                            sc.wait_ge(ld.h, 48 * (bh + 1))
                        vv.wait(sc)
                        vv.inc(sc.activation(q2[:, :], qb if which == 0 else kb, AF.Square))
                    @blk.tensor
                    def _(t, which=which):
                        vv.wait(t)
                        for j in range(L // 512):
                            pst = [psN, psN2][j % 2]
                            if j >= 2:
                                t.wait_ge(vv.h, vv.v + (j - 1))
                            mm.inc(t.matmul(pst[0:1, 0:512], onesc[:, :],
                                            q2[:, j * 512:(j + 1) * 512], start=True, stop=True))
                    @blk.vector
                    def _(v, which=which):
                        base = mm.v - L // 512
                        off = 0 if which == 0 else L
                        for j in range(L // 512):
                            pst = [psN, psN2][j % 2]
                            v.wait_ge(mm.h, base + j + 1)
                            vv.inc(v.tensor_copy(nrm[0:1, off + j * 512:off + (j + 1) * 512],
                                                 pst[0:1, 0:512]))
            with nc.Block() as blk:
                @blk.vector
                def _(v):
                    vv.inc(v.reduce_max(nrm[0:1, 2 * L:2 * L + 1], nrm[0:1, L:2 * L],
                                        axis=mybir.AxisListType.X))
                @blk.scalar
                def _(sc):
                    vv.wait(sc)
                    vv.inc(sc.activation(nrm[0:1, 2 * L + 2:3 * L + 2], nrm[0:1, 0:L], AF.Sqrt))
                    vv.wait(sc)
                    vv.inc(sc.activation(nrm[0:1, 2 * L + 1:2 * L + 2],
                                         nrm[0:1, 2 * L:2 * L + 1], AF.Sqrt))
                @blk.vector
                def _(v):
                    vv.wait(v)
                    vv.inc(v.tensor_scalar(nrm[0:1, 3 * L + 2:4 * L + 2],
                                           nrm[0:1, 2 * L + 2:3 * L + 2],
                                           nrm[0:1, 2 * L + 1:2 * L + 2], -1.0,
                                           op0=ALU.mult, op1=ALU.mult))
                    vv.wait(v)
                    vv.inc(v.tensor_copy(negm[0:1, :], nrm[0:1, 3 * L + 2:4 * L + 2]))

            for qc in range(nq):
                nkt = (qc + 1) * (QC // 128)
                with nc.Block() as blk:
                    @blk.tensor
                    def _(t, bh=bh, qc=qc, nkt=nkt, qb=qb, kb=kb):
                        vv.wait(t)
                        for kt_ in range(nkt):
                            pst = [psA, psB][kt_ % 2]
                            if kt_ >= 2:
                                t.wait_ge(ex.h, ex.v + kt_ - 1)
                            t.matmul(pst[:, 0:QC], kb[:, kt_ * 128:(kt_ + 1) * 128],
                                     qb[:, qc * QC:(qc + 1) * QC], start=True, stop=False)
                            mm.inc(t.matmul(pst[:, 0:QC], onesr[:, :],
                                            negm[0:1, qc * QC:(qc + 1) * QC],
                                            start=False, stop=True))
                    @blk.scalar
                    def _(sc, qc=qc, nkt=nkt):
                        mmb = mm.v - nkt
                        for kt_ in range(nkt):
                            pst = [psA, psB][kt_ % 2]
                            sc.wait_ge(mm.h, mmb + kt_ + 1)
                            ex.inc(sc.activation(pbuf[:, kt_ * QC:(kt_ + 1) * QC],
                                                 pst[:, 0:QC], AF.Exp, scale=SC))
                    @blk.gpsimd
                    def _(g, qc=qc, nkt=nkt):
                        exb = ex.v - nkt
                        for kt_ in range(max(0, nkt - 4), nkt):
                            g.wait_ge(ex.h, exb + kt_ + 1)
                            af.inc(g.affine_select(
                                pbuf[:, kt_ * QC:(kt_ + 1) * QC],
                                pbuf[:, kt_ * QC:(kt_ + 1) * QC], fill=0.0,
                                channel_multiplier=-1, base=qc * QC - kt_ * 128,
                                pattern=[[1, QC]], compare_op=ALU.is_ge))
                with nc.Block() as blk:
                    @blk.tensor
                    def _(t, bh=bh, qc=qc, nkt=nkt, vb_off=vb_off):
                        for sub in range(4):
                            for kt_ in range(nkt):
                                t.matmul(psO[:, sub * 128:(sub + 1) * 128],
                                         pbuf[:, kt_ * QC + sub * 128:kt_ * QC + (sub + 1) * 128],
                                         vs[:, vb_off + kt_ * HD:vb_off + (kt_ + 1) * HD],
                                         start=(kt_ == 0), stop=(kt_ == nkt - 1))
                            ins = None
                            for kt_ in range(nkt):
                                ins = t.matmul(psD[:, sub * 4:sub * 4 + 1],
                                               pbuf[:, kt_ * QC + sub * 128:kt_ * QC + (sub + 1) * 128],
                                               onesc[:, :], start=(kt_ == 0), stop=(kt_ == nkt - 1))
                            mm.inc(ins)
                    @blk.vector
                    def _(v, bh=bh, qc=qc):
                        mm.wait(v)
                        so.wait(v)
                        for sub in range(4):
                            vv.inc(v.reciprocal(rcp[:, sub:sub + 1], psD[:, sub * 4:sub * 4 + 1]))
                        vv.wait(v)
                        for sub in range(4):
                            vv.inc(v.tensor_scalar(ost[:, sub * 128:(sub + 1) * 128],
                                                   psO[:, sub * 128:(sub + 1) * 128],
                                                   rcp[:, sub:sub + 1], None, op0=ALU.mult))
                    @blk.sync
                    def _(s, bh=bh, qc=qc):
                        vv.wait(s)
                        so.inc(s.dma_start(
                            o_out[bh, qc * QC:(qc + 1) * QC, :].rearrange("(s p) c -> p s c", p=128),
                            ost[:, :].rearrange("p (s c) -> p s c", s=4)), 16)
        with nc.Block() as blk:
            @blk.sync
            def _(s):
                so.wait(s)
    return nc



def build_l1_tail(T):
    from contextlib import ExitStack
    HFD = 11008
    MF = HFD // 128
    KF1 = 1024
    KT1 = KF1 // 128
    K2P = 2816
    KT2 = K2P // 128
    mt = D // 128
    kt = D // 128
    tt = cdiv(T, 512)
    LS = 0.5
    nc = bass.Bass("TRN2", target_bir_lowering=False)
    attn_in = nc.dram_tensor("attn", [D, T], BF16, kind="ExternalInput").ap()
    xn_in = nc.dram_tensor("xn", [D, T], BF16, kind="ExternalInput").ap()
    x_in = nc.dram_tensor("x", [D, T], F32, kind="ExternalInput").ap()
    wo_in = nc.dram_tensor("wo", [mt, D, 128], F32, kind="ExternalInput").ap()
    w1_in = nc.dram_tensor("w1", [MF, KF1, 128], F32, kind="ExternalInput").ap()
    w3_in = nc.dram_tensor("w3", [MF, KF1, 128], F32, kind="ExternalInput").ap()
    w2_in = nc.dram_tensor("w2", [mt, K2P, 128], F32, kind="ExternalInput").ap()
    a1_in = nc.dram_tensor("a1", [D, 2], F32, kind="ExternalInput").ap()
    a3_in = nc.dram_tensor("a3", [D, 2], F32, kind="ExternalInput").ap()
    a2_in = nc.dram_tensor("a2", [HFD, 2], F32, kind="ExternalInput").ap()
    b1_in = nc.dram_tensor("b1", [2, HFD], F32, kind="ExternalInput").ap()
    b3_in = nc.dram_tensor("b3", [2, HFD], F32, kind="ExternalInput").ap()
    b2_in = nc.dram_tensor("b2", [2, D], F32, kind="ExternalInput").ap()
    x2_out = nc.dram_tensor("x2", [D, T], F32, kind="ExternalOutput").ap()
    s_scr = nc.dram_tensor("s_scr8", [D, T], F32, kind="Internal").ap()
    with ExitStack() as es:
        sbt = lambda nm, shp, dt_: es.enter_context(nc.sbuf_tensor(nm, shp, dt_))
        big = sbt("big_", [128, kt * T], BF16)
        xf = sbt("xf_", [128, KT1 * T], BF16)
        wa = sbt("wa_", [128, kt * 128], BF16)
        wb = sbt("wb_", [128, kt * 128], BF16)
        a1s = sbt("a1s_", [128, kt * 2], BF16)
        a3s = sbt("a3s_", [128, kt * 2], BF16)
        a2s = sbt("a2s_", [128, MF * 2], BF16)
        b1b = sbt("b1b_", [2, HFD], BF16)
        b3b = sbt("b3b_", [2, HFD], BF16)
        b2b = sbt("b2b_", [2, D], BF16)
        lar = sbt("lar_", [2, 6 * T], F32)
        larb = sbt("larb_", [2, 3 * T], BF16)
        gstp = sbt("gstp_", [128, T], BF16)
        ustp = sbt("ustp_", [128, T], BF16)
        hstp = sbt("hstp_", [128, T], BF16)
        hcp = sbt("hcp_", [128, T], BF16)
        sst = sbt("sst_", [128, T], F32)
        xrt = sbt("xrt_", [128, T], F32)
        xot = sbt("xot_", [128, T], F32)
        silt = sbt("silt_", [128, T], F32)
        ps = es.enter_context(nc.psum_tensor("ps", [128, 512], F32))
        ps2 = es.enter_context(nc.psum_tensor("ps2", [128, 512], F32))
        psL = es.enter_context(nc.psum_tensor("psL", [128, 512], F32))
        psL2 = es.enter_context(nc.psum_tensor("psL2", [128, 512], F32))
        psM = es.enter_context(nc.psum_tensor("psM", [128, 512], F32))
        psN = es.enter_context(nc.psum_tensor("psN", [128, 512], F32))
        lx = Sem(es, nc, "lx")
        lw = Sem(es, nc, "lw")
        mm = Sem(es, nc, "mm")
        ev = Sem(es, nc, "ev")
        so = Sem(es, nc, "so")
        vv = Sem(es, nc, "vv")

        HF_T = 2752

        with nc.Block() as blk:
            @blk.sync
            def _(s):
                for k in range(kt):
                    lx.inc(s.dma_start(big[:, k * T:(k + 1) * T], attn_in[k * 128:(k + 1) * 128, :]), 16)
        for m in range(mt):
            with nc.Block() as blk:
                @blk.gpsimd
                def _(g, m=m):
                    wf = [wa, wb][m % 2]
                    lw.inc(g.dma_start(wf[:, :].rearrange("p (k f) -> p k f", k=kt),
                                       wo_in[m, :, :].rearrange("(k p) f -> p k f", p=128)), 16)
                @blk.tensor
                def _(t, m=m):
                    if m == 0:
                        lx.wait(t)
                    lw.wait(t)
                    wf = [wa, wb][m % 2]
                    for j in range(tt):
                        f = min(512, T - j * 512)
                        pst = [ps, ps2][j % 2]
                        ins = None
                        for k in range(kt):
                            ins = t.matmul(pst[:, 0:f], wf[:, k * 128:(k + 1) * 128],
                                           big[:, k * T + j * 512:k * T + j * 512 + f],
                                           start=(k == 0), stop=(k == kt - 1))
                        mm.inc(ins)
                @blk.vector
                def _(v, m=m):
                    base = mm.v - tt
                    for j in range(tt):
                        f = min(512, T - j * 512)
                        pst = [ps, ps2][j % 2]
                        v.wait_ge(mm.h, base + j + 1)
                        ev.inc(v.tensor_copy(sst[:, j * 512:j * 512 + f], pst[:, 0:f]))
                @blk.sync
                def _(s, m=m):
                    ev.wait(s)
                    so.inc(s.dma_start(s_scr[m * 128:(m + 1) * 128, :], sst[:, :]), 16)

        with nc.Block() as blk:
            @blk.sync
            def _(s):
                so.wait(s)
                for k in range(kt):
                    lx.inc(s.dma_start(big[:, k * T:(k + 1) * T], xn_in[k * 128:(k + 1) * 128, :]), 16)
            @blk.gpsimd
            def _(g):
                lw.inc(g.dma_start(a1s[:, :].rearrange("p (k c) -> p k c", k=kt),
                                   a1_in.rearrange("(k p) c -> p k c", p=128)), 16)
                lw.inc(g.dma_start(a3s[:, :].rearrange("p (k c) -> p k c", k=kt),
                                   a3_in.rearrange("(k p) c -> p k c", p=128)), 16)
                lw.inc(g.dma_start(a2s[:, :].rearrange("p (k c) -> p k c", k=MF),
                                   a2_in.rearrange("(k p) c -> p k c", p=128)), 16)
                lw.inc(g.dma_start(b1b[:, :], b1_in), 16)
                lw.inc(g.dma_start(b3b[:, :], b3_in), 16)
                lw.inc(g.dma_start(b2b[:, :], b2_in), 16)
        with nc.Block() as blk:
            @blk.vector
            def _(v):
                lw.wait(v)
                so.wait(v)
                vv.inc(v.tensor_scalar(b1b[:, :], b1b[:, :], LS, None, op0=ALU.mult))
                vv.inc(v.tensor_scalar(b3b[:, :], b3b[:, :], LS, None, op0=ALU.mult))
                vv.inc(v.tensor_scalar(b2b[:, :], b2b[:, :], LS, None, op0=ALU.mult))
                lx.wait(v)
                for jt in range(KT1):
                    vv.inc(v.tensor_tensor(sst[:, 0:T], big[:, jt * T:(jt + 1) * T],
                                           big[:, (jt + 8) * T:(jt + 9) * T], op=ALU.add))
                    vv.wait(v)
                    vv.inc(v.tensor_tensor(xrt[:, 0:T], big[:, (jt + 16) * T:(jt + 17) * T],
                                           big[:, (jt + 24) * T:(jt + 25) * T], op=ALU.add))
                    vv.wait(v)
                    vv.inc(v.tensor_tensor(xf[:, jt * T:(jt + 1) * T], sst[:, 0:T],
                                           xrt[:, 0:T], op=ALU.add))
                    vv.wait(v)
            vv_evac0 = vv.v
            @blk.tensor
            def _(t):
                vv.wait(t)
                for which, aT in [(0, a1s), (1, a3s)]:
                    if which == 1:
                        t.wait_ge(vv.h, vv_evac0 + tt)
                    for j in range(tt):
                        f = min(512, T - j * 512)
                        pst = [psL, psL2][j % 2]
                        ins = None
                        for k in range(kt):
                            ins = t.matmul(pst[0:2, 0:f], aT[:, k * 2:(k + 1) * 2],
                                           big[:, k * T + j * 512:k * T + j * 512 + f],
                                           start=(k == 0), stop=(k == kt - 1))
                        mm.inc(ins)
            @blk.vector
            def _(v):
                base = mm.v - 2 * tt
                cnt = 0
                for which in range(2):
                    for j in range(tt):
                        f = min(512, T - j * 512)
                        pst = [psL, psL2][j % 2]
                        cnt += 1
                        v.wait_ge(mm.h, base + cnt)
                        vv.inc(v.tensor_copy(larb[0:2, which * T + j * 512:which * T + j * 512 + f],
                                             pst[0:2, 0:f]))
                        vv.wait(v)
        hf = big
        with nc.Block() as blk:
            @blk.vector
            def _(v):
                vv.inc(v.memset(hf[:, 0:KT2 * T], 0.0))

        for m in range(MF):
            with nc.Block() as blk:
                @blk.gpsimd
                def _(g, m=m):
                    lw.inc(g.dma_start(wa[:, 0:KT1 * 128].rearrange("p (k f) -> p k f", k=KT1),
                                       w1_in[m, :, :].rearrange("(k p) f -> p k f", p=128)), 16)
                    lw.inc(g.dma_start(wb[:, 0:KT1 * 128].rearrange("p (k f) -> p k f", k=KT1),
                                       w3_in[m, :, :].rearrange("(k p) f -> p k f", p=128)), 16)
                @blk.tensor
                def _(t, m=m):
                    lw.wait(t)
                    vv.wait(t)
                    for which, wf, bT in [(0, wa, b1b), (1, wb, b3b)]:
                        for j in range(tt):
                            f = min(512, T - j * 512)
                            pst = [[ps, ps2], [psM, psN]][which][j % 2]
                            for k in range(KT1):
                                t.matmul(pst[:, 0:f], wf[:, k * 128:(k + 1) * 128],
                                         xf[:, k * T + j * 512:k * T + j * 512 + f],
                                         start=(k == 0), stop=False)
                            mm.inc(t.matmul(pst[:, 0:f], bT[0:2, m * 128:(m + 1) * 128],
                                            larb[0:2, which * T + j * 512:which * T + j * 512 + f],
                                            start=False, stop=True))
                @blk.scalar
                def _(sc, m=m):
                    base = mm.v - 2 * tt
                    for j in range(tt):
                        f = min(512, T - j * 512)
                        sc.wait_ge(mm.h, base + j + 1)
                        vv.inc(sc.activation(gstp[:, j * 512:j * 512 + f],
                                             [ps, ps2][j % 2][:, 0:f], AF.Silu))
                @blk.vector
                def _(v, m=m):
                    base = mm.v - tt
                    for j in range(tt):
                        f = min(512, T - j * 512)
                        v.wait_ge(mm.h, base + j + 1)
                        vv.inc(v.tensor_copy(ustp[:, j * 512:j * 512 + f],
                                             [psM, psN][j % 2][:, 0:f]))
                    vv.wait(v)
                    vv.inc(v.tensor_tensor(hstp[:, :], gstp[:, :], ustp[:, :], op=ALU.mult))
                    vv.wait(v)
                    for seg in range(2):
                        fr = (128 * m + 64 * seg) % HF_T
                        c, o = fr // 128, fr % 128
                        sb_ = 64 * seg
                        if o == sb_:
                            srcv = hstp[sb_:sb_ + 64, :]
                        else:
                            vv.inc(v.tensor_copy(hcp[o:o + 64, :], hstp[sb_:sb_ + 64, :]))
                            vv.wait(v)
                            srcv = hcp[o:o + 64, :]
                        vv.inc(v.tensor_tensor(hf[o:o + 64, c * T:(c + 1) * T],
                                               hf[o:o + 64, c * T:(c + 1) * T],
                                               srcv, op=ALU.add))
                        vv.wait(v)
                @blk.tensor
                def _(t, m=m):
                    vv.wait(t)
                    for j in range(tt):
                        f = min(512, T - j * 512)
                        ins = t.matmul([psL, psL2][j % 2][0:2, 0:f],
                                       a2s[:, m * 2:(m + 1) * 2],
                                       hstp[:, j * 512:j * 512 + f],
                                       start=(m == 0), stop=(m == MF - 1))
                        if m == MF - 1:
                            mm.inc(ins)
        with nc.Block() as blk:
            @blk.vector
            def _(v):
                mm.wait(v)
                for j in range(tt):
                    f = min(512, T - j * 512)
                    vv.inc(v.tensor_copy(larb[0:2, 2 * T + j * 512:2 * T + j * 512 + f],
                                         [psL, psL2][j % 2][0:2, 0:f]))

        for m in range(mt):
            r = slice(m * 128, (m + 1) * 128)
            with nc.Block() as blk:
                @blk.gpsimd
                def _(g, m=m):
                    wf = [wa, wb][m % 2]
                    lw.inc(g.dma_start(wf[:, 0:KT2 * 128].rearrange("p (k f) -> p k f", k=KT2),
                                       w2_in[m, :, :].rearrange("(k p) f -> p k f", p=128)), 16)
                @blk.sync
                def _(s, r=r, m=m):
                    so.inc(s.dma_start(sst[:, :], s_scr[r, :]), 16)
                    so.inc(s.dma_start(xrt[:, :], x_in[r, :]), 16)
                @blk.tensor
                def _(t, m=m):
                    lw.wait(t)
                    vv.wait(t)
                    wf = [wa, wb][m % 2]
                    for j in range(tt):
                        f = min(512, T - j * 512)
                        pst = [ps, ps2][j % 2]
                        for k in range(KT2):
                            t.matmul(pst[:, 0:f], wf[:, k * 128:(k + 1) * 128],
                                     hf[:, k * T + j * 512:k * T + j * 512 + f],
                                     start=(k == 0), stop=False)
                        mm.inc(t.matmul(pst[:, 0:f], b2b[0:2, m * 128:(m + 1) * 128],
                                        larb[0:2, 2 * T + j * 512:2 * T + j * 512 + f],
                                        start=False, stop=True))
                @blk.scalar
                def _(sc, m=m):
                    base = mm.v - tt
                    for j in range(tt):
                        f = min(512, T - j * 512)
                        pst = [ps, ps2][j % 2]
                        sc.wait_ge(mm.h, base + j + 1)
                        vv.inc(sc.activation(silt[:, j * 512:j * 512 + f], pst[:, 0:f], AF.Silu))
                @blk.vector
                def _(v, m=m):
                    vv.wait(v)
                    so.wait(v)
                    vv.inc(v.tensor_tensor(silt[:, :], sst[:, :], silt[:, :], op=ALU.mult))
                    vv.wait(v)
                    vv.inc(v.tensor_tensor(xot[:, :], xrt[:, :], silt[:, :], op=ALU.add))
                @blk.sync
                def _(s, r=r):
                    vv.wait(s)
                    so.inc(s.dma_start(x2_out[r, :], xot[:, :]), 16)
        with nc.Block() as blk:
            @blk.sync
            def _(s):
                so.wait(s)
    return nc



def tile_w(wT, Mpad=None, Kpad=None, tile=128):
    wT = np.asarray(wT, np.float32)
    Kd, M = wT.shape
    if Kpad and Kd < Kpad:
        wT = np.concatenate([wT, np.zeros((Kpad - Kd, M), np.float32)], axis=0)
        Kd = Kpad
    if Mpad and M < Mpad:
        wT = np.concatenate([wT, np.zeros((Kd, Mpad - M), np.float32)], axis=1)
        M = Mpad
    mt = M // tile
    return np.ascontiguousarray(wT.reshape(Kd, mt, tile).transpose(1, 0, 2))


def run8(nc, in_maps):
    res = run_bass_kernel_spmd(nc, in_maps, core_ids=list(range(NCORES)))
    return res.results


_PERM = np.concatenate([np.arange(0, 128, 2), np.arange(1, 128, 2)])


def _perm_rows(w):
    idx = np.concatenate([h * 128 + _PERM for h in range(32)])
    return w[idx]


def kernel(**inputs):
    inp = {k: np.asarray(v) for k, v in inputs.items()}
    C = np.ascontiguousarray

    x_flat = inp["L0_attn_hidden_states"].reshape(NTOK, D)
    nc = build_ln_tm(TPC)
    res = run8(nc, [{"x": C(x_flat[c * TPC:(c + 1) * TPC])} for c in range(8)])
    xn = np.concatenate([r["xn"] for r in res], 0)

    xn_fm = C(xn.T)
    wt = tile_w(C(inp["L0_attn_in_proj_w"].T), Mpad=10368)
    nc = build_mm_fm(D, 10368, TPC)
    res = run8(nc, [{"x": C(xn_fm[:, c * TPC:(c + 1) * TPC]), "w": wt} for c in range(8)])
    proj = np.concatenate([r["o"][:PROJ, :] for r in res], 1)

    hbc = proj[I_DIM:I_DIM + CONV_DIM]
    dt = proj[I_DIM + CONV_DIM:]
    nc = build_conv_dt(TPC)
    maps = []
    for c in range(8):
        t0, t1 = c * TPC, (c + 1) * TPC
        halo = np.zeros((CONV_DIM, 3), np.float32) if t0 % SEQ == 0 else hbc[:, t0 - 3:t0]
        maps.append({"hbc": C(np.concatenate([halo, hbc[:, t0:t1]], 1)),
                     "cw": inp["L0_attn_conv_weight"], "cb": inp["L0_attn_conv_bias"],
                     "dt": C(dt[:, t0:t1]), "dtb": inp["L0_attn_dt_bias"]})
    res = run8(nc, maps)
    xbc = np.concatenate([r["xbc"] for r in res], 1)
    dtp = np.concatenate([r["dtp"] for r in res], 1)

    xbc3 = xbc.reshape(CONV_DIM, 2, SEQ)
    dtp3 = dtp.reshape(H, 2, SEQ)
    ident = np.eye(128, dtype=np.float32)
    nc = build_scan()
    maps = []
    for c in range(8):
        bfm = C(xbc3[I_DIM + c * 128:I_DIM + (c + 1) * 128].transpose(1, 0, 2))
        cfm = C(xbc3[I_DIM + GN + c * 128:I_DIM + GN + (c + 1) * 128].transpose(1, 0, 2))
        btm = C(bfm.transpose(0, 2, 1))
        xp = C(xbc3[c * 512:(c + 1) * 512].transpose(1, 2, 0))
        dtph = C(dtp3[c * 8:(c + 1) * 8].transpose(1, 0, 2).reshape(16, SEQ))
        maps.append({"bfm": bfm, "cfm": cfm, "btm": btm, "xp": xp, "dtph": dtph,
                     "dtpt": C(dtph.T),
                     "alog": C(np.concatenate([inp["L0_attn_A_log"][c * 8:(c + 1) * 8]] * 2)),
                     "identf": ident})
    res = run8(nc, maps)
    ys = np.stack([r["y2"].reshape(SEQ, 2, 8, P) for r in res], 2)
    ys = ys.transpose(1, 0, 2, 3, 4).reshape(2, SEQ, H * P)

    y_tm = C(ys.reshape(NTOK, D))
    xp_tm = C(xbc[:I_DIM].T)
    gate_tm = C(proj[:I_DIM].T)
    drep = C(np.repeat(inp["L0_attn_D"], P))
    nc = build_gnorm_gate(TPC)
    maps = [{"y": C(y_tm[c * TPC:(c + 1) * TPC]), "xp": C(xp_tm[c * TPC:(c + 1) * TPC]),
             "gate": C(gate_tm[c * TPC:(c + 1) * TPC]), "drep": drep,
             "nw": inp["L0_attn_norm_weight"]} for c in range(8)]
    res = run8(nc, maps)
    yn = np.concatenate([r["yn"] for r in res], 0)

    yn_fm = C(yn.T)
    x_fm = C(x_flat.T)
    wo = tile_w(C(inp["L0_attn_out_proj_w"].T))
    wf0 = tile_w(C(inp["L0_ffn_weight"].T))
    nc = build_l0_tail(TPC)
    maps = [{"yn": C(yn_fm[:, c * TPC:(c + 1) * TPC]), "xn": C(xn_fm[:, c * TPC:(c + 1) * TPC]),
             "x": C(x_fm[:, c * TPC:(c + 1) * TPC]), "wo": wo, "wf": wf0} for c in range(8)]
    res = run8(nc, maps)
    x1 = np.concatenate([r["x1"] for r in res], 1)

    x1_tm = C(x1.T)
    nc = build_ln_tm(TPC)
    res = run8(nc, [{"x": C(x1_tm[c * TPC:(c + 1) * TPC])} for c in range(8)])
    xn1 = np.concatenate([r["xn"] for r in res], 0)

    xn1_fm = C(xn1.T)
    wq = tile_w(C(_perm_rows(inp["L1_attn_wq"]).T))
    wk = tile_w(C(_perm_rows(inp["L1_attn_wk"]).T))
    wv = tile_w(C(inp["L1_attn_wv"].T), tile=512)
    nc = build_qkv(TPC)
    maps = []
    for c in range(8):
        pos = (np.arange(c * TPC, (c + 1) * TPC) % SEQ).astype(np.float32)
        maps.append({"xn": C(xn1_fm[:, c * TPC:(c + 1) * TPC]), "wq": wq, "wk": wk,
                     "wv": wv, "pos": pos})
    res = run8(nc, maps)
    qr = np.concatenate([r["qr"] for r in res], 1)
    kr = np.concatenate([r["kr"] for r in res], 1)
    vt = np.concatenate([r["vt"] for r in res], 0)

    q3 = qr.reshape(32, 128, 2, SEQ)
    k3 = kr.reshape(32, 128, 2, SEQ)
    v3 = vt.reshape(2, SEQ, 32, 128)
    nc = build_attn()
    maps = []
    for c in range(8):
        bhs = [(bh // 2, bh % 2) for bh in range(8 * c, 8 * c + 8)]
        maps.append({"q": C(np.stack([q3[h, :, b, :] for (h, b) in bhs], 0)),
                     "k": C(np.stack([k3[h, :, b, :] for (h, b) in bhs], 0)),
                     "v": C(np.stack([v3[b, :, h, :] for (h, b) in bhs], 0))})
    res = run8(nc, maps)
    att = np.stack([r["o"] for r in res], 0).reshape(32, 2, SEQ, 128).transpose(1, 2, 0, 3)

    att_fm = C(att.reshape(NTOK, D).T)
    wo1 = tile_w(C(inp["L1_attn_wo"].T))
    w1 = tile_w(C(inp["L1_ffn_w1q"].T))
    w3 = tile_w(C(inp["L1_ffn_w3q"].T))
    w2 = tile_w(C(inp["L1_ffn_w2q"].T), Kpad=2816)
    nc = build_l1_tail(TPC)
    maps = []
    for c in range(8):
        r = slice(c * TPC, (c + 1) * TPC)
        maps.append({"attn": C(att_fm[:, r]), "xn": C(xn1_fm[:, r]), "x": C(x1[:, r]),
                     "wo": wo1, "w1": w1, "w3": w3, "w2": w2,
                     "a1": C(inp["L1_ffn_a1"].T), "a3": C(inp["L1_ffn_a3"].T),
                     "a2": C(inp["L1_ffn_a2"].T), "b1": C(inp["L1_ffn_b1"].T),
                     "b3": C(inp["L1_ffn_b3"].T), "b2": C(inp["L1_ffn_b2"].T)})
    res = run8(nc, maps)
    x2 = np.concatenate([r["x2"] for r in res], 1)
    return np.ascontiguousarray(x2.T.reshape(2, SEQ, D)).astype(np.float32)
